# version 16
# baseline (speedup 1.0000x reference)
import contextlib
import os

import ml_dtypes
import numpy as np

import concourse.bass as bass
import concourse.mybir as mybir
from concourse.bass_utils import run_bass_kernel_spmd

F32 = mybir.dt.float32
BF16 = mybir.dt.bfloat16
ALU = mybir.AluOpType
AFT = mybir.ActivationFunctionType
AX = mybir.AxisListType

B, T, D = 16, 4096, 1024
H, HD = 16, 64
NCORES = 8
TILE = 128
FCH = D // 128
SCALE_Q = float(HD) ** -0.5
LN_EPS = 1e-5
TRACE_SCHED = None


def _host_tables(nt):
    import jax
    import jax.numpy as jnp
    with jax.default_device(jax.local_devices(backend="cpu")[0]):
        inv = 1.0 / 10000.0 ** (jnp.arange(0, HD, 2, dtype=jnp.float32) / HD)
        t = jnp.arange(nt * TILE, dtype=jnp.float32)
        freqs = t[:, None] * inv[None, :]
        cos = np.asarray(jnp.cos(freqs), dtype=np.float32)
        sin = np.asarray(jnp.sin(freqs), dtype=np.float32)

    def fold(a):
        a = a.reshape(nt, TILE, 32)
        return np.ascontiguousarray(a.transpose(1, 0, 2).reshape(TILE, nt * 32))
    return fold(cos), fold(sin)


_DRAIN_ALL = (
    "tensor_copy", "tensor_tensor", "tensor_scalar", "tensor_single_scalar",
    "scalar_tensor_tensor", "tensor_add", "tensor_sub", "tensor_mul",
    "tensor_max", "reduce_max", "reduce_sum", "tensor_reduce", "reciprocal",
    "memset", "activation", "dma_start", "select",
)
_DRAIN_SETS = {
    "all": frozenset(_DRAIN_ALL),
    "dma": frozenset(("dma_start",)),
    "dma_ts": frozenset(("dma_start", "tensor_scalar", "tensor_single_scalar",
                         "scalar_tensor_tensor", "activation")),
    "x1": frozenset(("dma_start", "tensor_scalar", "tensor_single_scalar",
                     "scalar_tensor_tensor")),
    "x2": frozenset(("dma_start", "activation")),
    "none": frozenset(()),
}
_DRAIN_SETS["lean"] = frozenset((
    "dma_start", "memset", "tensor_copy", "reduce_max", "reduce_sum",
    "tensor_reduce", "reciprocal", "select",
))
_DRAIN_SETS["lean_tt"] = _DRAIN_SETS["lean"] | frozenset(("tensor_tensor",))
_DRAIN_SETS["lean_act"] = _DRAIN_SETS["lean"] | frozenset(("activation",))
_DRAIN_SETS["lean_tt_act"] = _DRAIN_SETS["lean_tt"] | frozenset(("activation",))
_DRAIN_SETS["lean_sst"] = _DRAIN_SETS["lean"] | frozenset(
    ("scalar_tensor_tensor", "tensor_scalar", "tensor_single_scalar"))
_IS_SIM = (os.environ.get("BASS_SIM_FP64") == "1"
           or os.environ.get("K_SIMCHECK") == "1")
_DRAIN_DEFAULT = "all" if _IS_SIM else "lean_sst"
_DRAIN_OPS = _DRAIN_SETS.get(os.environ.get("K_DRAINS", _DRAIN_DEFAULT),
                             _DRAIN_SETS[_DRAIN_DEFAULT])


class _Eng:

    def __init__(self, real, auto_drain=False, sim_drain=False):
        self._r = real
        self._ad = auto_drain
        self._sd = sim_drain
        self._nd = None

    def nd(self):
        if self._nd is None:
            self._nd = _Eng(self._r, auto_drain=self._sd)
        return self._nd

    def __getattr__(self, n):
        if self._r is None:
            return lambda *a, **k: None
        f = getattr(self._r, n)
        if self._ad and n in _DRAIN_OPS:
            r = self._r

            def wrapped(*a, **k):
                r.drain()
                return f(*a, **k)
            return wrapped
        return f


def _ti(ins, semh, n=1):
    if ins is not None:
        ins.then_inc(semh, n)


def build_nc(NT=T // TILE, n_seq=2):
    NTOT = NT * n_seq
    TOK = NTOT * TILE
    nc = bass.Bass("TRN2", target_bir_lowering=False, debug=False)

    x_d = nc.dram_tensor("x", [TOK, D], F32, kind="ExternalInput").ap()
    wqkvT_d = nc.dram_tensor("wqkvT", [D, 3 * D], F32, kind="ExternalInput").ap()
    woutT_d = nc.dram_tensor("woutT", [D, D], F32, kind="ExternalInput").ap()
    rhs_d = nc.dram_tensor("rhs", [D, D], F32, kind="ExternalInput").ap()
    rhsT_d = nc.dram_tensor("rhsT", [D, D], F32, kind="ExternalInput").ap()
    cos_d = nc.dram_tensor("cos", [TILE, NT * 32], F32, kind="ExternalInput").ap()
    sin_d = nc.dram_tensor("sin", [TILE, NT * 32], F32, kind="ExternalInput").ap()
    idn_d = nc.dram_tensor("idn", [128, 128], BF16, kind="ExternalInput").ap()
    idnf_d = nc.dram_tensor("idnf", [128, 128], F32, kind="ExternalInput").ap()
    idsh_d = nc.dram_tensor("idsh", [64, 128], F32, kind="ExternalInput").ap()
    out_d = nc.dram_tensor("out", [TOK, D], F32, kind="ExternalOutput").ap()
    scr_xh = nc.dram_tensor("scr_xh", [NTOT, TILE, D], BF16, kind="Internal").ap()
    scr_xl = nc.dram_tensor("scr_xl", [NTOT, TILE, D], BF16, kind="Internal").ap()
    scr_ql = nc.dram_tensor("scr_ql", [NTOT, TILE, D], BF16, kind="Internal").ap()

    st_ = contextlib.ExitStack()
    with st_:
        def sb(name, shape, dt=F32):
            return st_.enter_context(nc.sbuf_tensor(name, shape, dt))

        wa = sb("wa", [128, FCH * 2048], BF16)
        wb = sb("wb", [128, FCH * 2048], BF16)
        qrhs = sb("qrhs", [128, FCH * D], BF16)
        cos_sb = sb("cos_sb", [TILE, NT * 32])
        sin_sb = sb("sin_sb", [TILE, NT * 32])
        idn = sb("idn_s", [128, 128], BF16)
        idnf = sb("idnf_s", [128, 128])
        idsh = sb("idsh_s", [64, 128])
        ones_col = sb("ones_col", [128, 1])
        ks_acc = sb("ks_acc", [128, D])
        kvstg = sb("kvstg", [64, 1024])
        ones1 = sb("ones1", [1, 128])
        rs_bc = sb("rs_bc", [128, D])
        kv_h = sb("kv_h", [128, n_seq * 8 * 130], BF16)
        kv_l = sb("kv_l", [128, n_seq * 8 * 130], BF16)
        ls_sb = sb("ls_sb", [128, NTOT])
        xbuf = [sb(f"xbuf{p}", [TILE, D]) for p in range(2)]
        xnbuf = [sb(f"xn{p}", [TILE, D]) for p in range(2)]
        xhtok = [sb(f"xht{p}", [TILE, D], BF16) for p in range(2)]
        xltok = [sb(f"xlt{p}", [TILE, D], BF16) for p in range(2)]
        xTh = [sb(f"xTh{p}", [128, D], BF16) for p in range(2)]
        xTl = [sb(f"xTl{p}", [128, D], BF16) for p in range(2)]
        kq = [sb(f"kq{p}", [TILE, D]) for p in range(2)]
        rot = [sb(f"rot{p}", [TILE, D]) for p in range(2)]
        vex = [sb(f"vex{p}", [TILE, 16 * 65]) for p in range(2)]
        ebuf = [sb(f"e{p}", [TILE, D]) for p in range(2)]
        qlhs = [sb(f"qlhs{p}", [TILE, D], BF16) for p in range(2)]
        stat = [sb(f"stat{p}", [TILE, 40]) for p in range(2)]
        qfTh_s = sb("qfTh_s", [128, D], BF16)
        qfTl_s = sb("qfTl_s", [128, D], BF16)
        qfTh = [qfTh_s, qfTh_s]
        qfTl = [qfTl_s, qfTl_s]
        upl = sb("upl", [TILE, D])
        vpl = sb("vpl", [TILE, D])
        vpl2 = sb("vpl2", [TILE, D])
        cpl = sb("cpl", [TILE, D])
        tpl = sb("tpl", [TILE, D])
        gtmp = sb("gtmp", [TILE, 512])
        csteps = sb("csteps", [TILE, 1])
        ff_sb = [upl, vpl]
        outb = [vpl2, cpl]
        rr_bc = xnbuf[0]

        pb = [st_.enter_context(nc.psum_tensor(f"pb{j}", [128, 512], F32)) for j in range(8)]

        s_x = st_.enter_context(nc.semaphore("s_x"))
        s_w = st_.enter_context(nc.semaphore("s_w"))
        s_scr = st_.enter_context(nc.semaphore("s_scr"))
        s_bx = st_.enter_context(nc.semaphore("s_bx"))
        s_out = st_.enter_context(nc.semaphore("s_out"))
        v_m = st_.enter_context(nc.semaphore("v_m"))
        s_m = st_.enter_context(nc.semaphore("s_m"))
        p_m = st_.enter_context(nc.semaphore("p_m"))
        g_m = st_.enter_context(nc.semaphore("g_m"))

        block = st_.enter_context(nc.Block())

        _ad = (_IS_SIM or os.environ.get("K_DRAINS", _DRAIN_DEFAULT) != "native")

        _simd = _IS_SIM

        def walk(eng, real):
            v = _Eng(real if eng == "v" else None, auto_drain=_ad, sim_drain=_simd)
            s = _Eng(real if eng == "s" else None, auto_drain=_ad, sim_drain=_simd)
            p = _Eng(real if eng == "p" else None)
            g = _Eng(real if eng == "g" else None, auto_drain=_ad, sim_drain=_simd)
            y = _Eng(real if eng == "y" else None)
            C = {"v": 0, "s": 0, "p": 0, "g": 0}
            DM = {"x": 0, "w": 0, "scr": 0, "bx": 0, "out": 0}
            SEMH = {"v": v_m, "s": s_m, "p": p_m, "g": g_m}
            DSEM = {"x": s_x, "w": s_w, "scr": s_scr, "bx": s_bx, "out": s_out}
            M = {}

            def bump(k, name=None):
                C[k] += 1
                if TRACE_SCHED is not None and eng == k:
                    TRACE_SCHED.append(("bump", eng, k, C[k], name))
                if name is not None:
                    M[name] = C[k]
                return C[k]

            def dbump(k, name=None):
                DM[k] += 16
                if name is not None:
                    M[name] = DM[k]
                return DM[k]

            def wv(e, k, name):
                if TRACE_SCHED is not None and e._r is not None:
                    TRACE_SCHED.append(("wait", eng, k, M[name], name))
                e.wait_ge(SEMH[k], M[name])

            def wd(e, k, name):
                e.wait_ge(DSEM[k], M[name])

            def dma_grp(e, k, transfers, name=None):
                e.wait_ge(DSEM[k], DM[k])
                for out_ap, in_ap in transfers:
                    _ti(e.dma_start(out=out_ap, in_=in_ap), DSEM[k], 16)
                    DM[k] += 16
                if name is not None:
                    M[name] = DM[k]

            def rope(e, ek, src, dst, jt, mark):
                co = cos_sb[:, 32 * jt:32 * jt + 32]
                si = sin_sb[:, 32 * jt:32 * jt + 32]
                cob = bass.AP(co.tensor, co.offset, [list(co.ap[0]), [0, H], list(co.ap[1])])
                sib = bass.AP(si.tensor, si.offset, [list(si.ap[0]), [0, H], list(si.ap[1])])
                sv = src[:].rearrange("p (h two d) -> p h two d", two=2, d=32)
                dv = dst[:].rearrange("p (h two d) -> p h two d", two=2, d=32)
                gv = gtmp[:].rearrange("p (h d) -> p h d", d=32)
                se, so = sv[:, :, 0, :], sv[:, :, 1, :]
                de, do = dv[:, :, 0, :], dv[:, :, 1, :]
                ge, go = gv, gv
                e.tensor_tensor(de, se, cob, ALU.mult)
                e.tensor_tensor(ge, so, sib, ALU.mult)
                e.tensor_tensor(de, de, ge, ALU.subtract)
                e.tensor_tensor(do, se, sib, ALU.mult)
                e.tensor_tensor(go, so, cob, ALU.mult)
                _ti(e.tensor_tensor(do, do, go, ALU.add), SEMH[ek], 1)
                bump(ek, mark)

            def peel_core(e, y_ap):
                e.tensor_scalar(vpl[:], y_ap, 1.0, 128.5, ALU.mult, ALU.add)
                src_, dst_ = vpl, vpl2
                bit = 128.0
                en = e.nd()
                while bit >= 1.0:
                    en.tensor_single_scalar(cpl[:], src_[:], bit, ALU.is_ge)
                    en.scalar_tensor_tensor(dst_[:], cpl[:], -bit, src_[:], ALU.mult, ALU.add)
                    src_, dst_ = dst_, src_
                    bit /= 2.0
                return src_

            dma_grp(y, "w", [(cos_sb[:], cos_d), (sin_sb[:], sin_d),
                             (idn[:], idn_d), (idnf[:], idnf_d),
                             (idsh[:], idsh_d)], "w_consts")

            v.memset(ones1[:], 1.0)
            v.memset(ones_col[:], 1.0)
            v.memset(ks_acc[:], 0.0)
            v.memset(kv_h[:], 0.0)
            v.memset(csteps[:], LN_EPS)
            _ti(v.memset(kv_l[:], 0.0), v_m, 1)
            bump("v", "v_init")

            for c in range(FCH):
                if c >= 2:
                    wv(y, "v", f"wkv_lo{c - 2}")
                base = 128 * c
                dma_grp(y, "w", [(xbuf[c % 2][:], wqkvT_d[base:base + 128, D:2 * D]),
                                 (xnbuf[c % 2][:], wqkvT_d[base:base + 128, 2 * D:3 * D])],
                        f"w_kv{c}")
                wd(s, "w", f"w_kv{c}")
                s.activation(wa[:, c * 2048:c * 2048 + 1024], xbuf[c % 2][:], AFT.Copy)
                _ti(s.activation(wa[:, c * 2048 + 1024:c * 2048 + 2048], xnbuf[c % 2][:], AFT.Copy),
                    s_m, 1)
                bump("s", f"wkv_hi{c}")
                wv(v, "s", f"wkv_hi{c}")
                v.tensor_tensor(wb[:, c * 2048:c * 2048 + 1024], xbuf[c % 2][:],
                                wa[:, c * 2048:c * 2048 + 1024], ALU.subtract)
                _ti(v.tensor_tensor(wb[:, c * 2048 + 1024:c * 2048 + 2048], xnbuf[c % 2][:],
                                    wa[:, c * 2048 + 1024:c * 2048 + 2048], ALU.subtract),
                    v_m, 1)
                bump("v", f"wkv_lo{c}")

            for c in range(FCH):
                if c >= 2:
                    wv(y, "v", f"ramax{c - 2}")
                dma_grp(y, "w", [(ebuf[c % 2][:], rhsT_d[128 * c:128 * (c + 1), :])], f"w_rT{c}")
                wd(v, "w", f"w_rT{c}")
                _ti(v.reduce_max(stat[0][:, 8 + c:9 + c], ebuf[c % 2][:], AX.X,
                                 apply_absolute_value=True), v_m, 1)
                bump("v", f"ramax{c}")
            v.drain()
            v.tensor_single_scalar(stat[0][:, 16:24], stat[0][:, 8:16], 1e-12, ALU.max)
            v.tensor_single_scalar(stat[1][:, 8:16], stat[0][:, 16:24], 1.0 / 127.0, ALU.mult)
            v.reciprocal(stat[1][:, 16:24], stat[0][:, 16:24])
            v.drain()
            _ti(v.tensor_single_scalar(stat[0][:, 0:8], stat[1][:, 16:24], 127.0, ALU.mult),
                v_m, 1)
            bump("v", "rsc")
            wv(p, "v", "rsc")
            wd(p, "w", "w_consts")
            for r in range(FCH):
                bank = pb[0] if r < 4 else pb[2]
                p.matmul(bank[0:1, 128 * (r % 4):128 * (r % 4) + 128], stat[1][:, 8 + r:9 + r],
                         idnf[:], start=True, stop=True)
            mm = None
            for r in range(FCH):
                bank = pb[1] if r < 4 else pb[3]
                mm = p.matmul(bank[0:1, 128 * (r % 4):128 * (r % 4) + 128], stat[0][:, 0 + r:1 + r],
                              idnf[:], start=True, stop=True)
            _ti(mm, p_m, 1)
            bump("p", "rowsT")
            wv(v, "p", "rowsT")
            v.nd().tensor_copy(upl[0:1, 0:512], pb[0][0:1, 0:512])
            v.nd().tensor_copy(upl[0:1, 512:1024], pb[2][0:1, 0:512])
            v.nd().tensor_copy(vpl[0:1, 0:512], pb[1][0:1, 0:512])
            _ti(v.nd().tensor_copy(vpl[0:1, 512:1024], pb[3][0:1, 0:512]), v_m, 1)
            bump("v", "rows_ev")
            wv(p, "v", "rows_ev")
            p.matmul(pb[2][:], ones1[:], upl[0:1, 0:512], start=True, stop=True)
            p.matmul(pb[3][:], ones1[:], upl[0:1, 512:1024], start=True, stop=True)
            p.matmul(pb[4][:], ones1[:], vpl[0:1, 0:512], start=True, stop=True)
            _ti(p.matmul(pb[5][:], ones1[:], vpl[0:1, 512:1024], start=True, stop=True),
                p_m, 1)
            bump("p", "rows_bc")
            wv(v, "p", "rows_bc")
            v.nd().tensor_copy(rs_bc[:, 0:512], pb[2][:])
            v.nd().tensor_copy(rs_bc[:, 512:1024], pb[3][:])
            v.nd().tensor_copy(rr_bc[:, 0:512], pb[4][:])
            _ti(v.nd().tensor_copy(rr_bc[:, 512:1024], pb[5][:]), v_m, 1)
            bump("v", "rsbc")

            for c in range(FCH):
                if c >= 2:
                    wv(y, "v", f"qrhs{c - 2}")
                else:
                    wv(y, "v", "rsbc")
                dma_grp(y, "w", [(ebuf[c % 2][:], rhs_d[128 * c:128 * (c + 1), :])], f"w_r{c}")
                wd(v, "w", f"w_r{c}")
                v.tensor_tensor(upl[:], ebuf[c % 2][:], rr_bc[:], ALU.mult)
                rem = peel_core(v, upl[:])
                v.nd().scalar_tensor_tensor(tpl[:], rem[:], -1.0, upl[:], ALU.mult, ALU.add)
                _ti(v.nd().tensor_scalar(qrhs[:, bass.ts(c, D)], tpl[:], 1.0, 0.5, ALU.mult, ALU.add),
                    v_m, 1)
                bump("v", f"qrhs{c}")

            for it in range(NTOT + 3):
                iA, iB, iD, iC = it, it - 1, it - 2, it - 3
                pA, pB_, pD, pC = iA % 2, iB % 2, iD % 2, iC % 2
                if 0 <= iA < NTOT:
                    if iA >= 2:
                        wv(y, "v", f"q{iA - 2}")
                    dma_grp(y, "x", [(xbuf[pA][:], x_d[TILE * iA:TILE * (iA + 1), :])], f"x{iA}")
                if 0 <= iB < NTOT:
                    wv(p, "v", f"split{iB}")
                    if iB >= 1:
                        wv(p, "v", f"qTe{iB - 1}")
                    if iB >= 2 and (iB - 2) % NT == 0 and iB > 2:
                        wv(p, "v", f"kvs{(iB - 2) // NT - 1}")
                    mm = None
                    for c in range(FCH):
                        bank = pb[2] if c < 4 else pb[3]
                        mm = p.matmul(bank[:, (c % 4) * 128:(c % 4) * 128 + 128],
                                      xhtok[pB_][:, bass.ts(c, 128)], idn[:],
                                      start=True, stop=True)
                    _ti(mm, p_m, 1)
                    bump("p", f"trh{iB}")
                    if iB >= 1:
                        wv(p, "v", f"trl{iB - 1}")
                    if iB >= 2:
                        wv(p, "s", f"pre{iB - 2}_2")
                        wv(p, "s", f"pre{iB - 2}_3")
                    mm = None
                    for c in range(FCH):
                        bank = pb[4] if c < 4 else pb[5]
                        mm = p.matmul(bank[:, (c % 4) * 128:(c % 4) * 128 + 128],
                                      xltok[pB_][:, bass.ts(c, 128)], idn[:],
                                      start=True, stop=True)
                    _ti(mm, p_m, 1)
                    bump("p", f"ptrl{iB}")
                if 0 <= iB < NTOT:
                    wv(s, "p", f"trh{iB}")
                    if iB >= 2:
                        wd(s, "scr", f"scrx{iB - 2}")
                    s.activation(xTh[pB_][:, 0:512], pb[2][:], AFT.Copy)
                    _ti(s.activation(xTh[pB_][:, 512:1024], pb[3][:], AFT.Copy), s_m, 1)
                    bump("s", f"trhe{iB}")
                if 0 <= iB < NTOT:
                    wv(v, "p", f"ptrl{iB}")
                    if iB >= 2:
                        wd(v, "scr", f"scrx{iB - 2}")
                    v.nd().tensor_copy(xTl[pB_][:, 0:512], pb[4][:])
                    _ti(v.nd().tensor_copy(xTl[pB_][:, 512:1024], pb[5][:]), v_m, 1)
                    bump("v", f"trl{iB}")
                if 0 <= iC < NTOT:
                    wv(v, "s", f"exp{iC}")
                    wv(v, "g", f"rope{iC}")
                    _ti(v.nd().scalar_tensor_tensor(kq[pC][:], ebuf[pC][:], 1.0, rot[pC][:],
                                                    ALU.min, ALU.add), v_m, 1)
                    bump("v", f"feat{iC}")
                if 0 <= iC < NTOT:
                    wv(p, "v", f"feat{iC}")
                    if iC % NT == 0 and iC > 0:
                        wv(p, "v", f"kvs{iC // NT - 1}")
                    jj = iC % NT
                    mm = None
                    for h in range(H):
                        bank = pb[6] if h < 8 else pb[7]
                        off = (h % 8) * 64
                        mm = p.matmul(bank[0:64, off:off + 64],
                                      kq[pC][:, h * 64:(h + 1) * 64],
                                      vex[pC][:, h * 64:(h + 1) * 64],
                                      start=(jj == 0 and h % 8 == 0),
                                      stop=(jj == NT - 1 and h % 8 == 7))
                    _ti(mm, p_m, 1)
                    bump("p", f"kv{iC}")
                    _ti(v.nd().tensor_tensor(ks_acc[:], ks_acc[:], kq[pC][:], ALU.add),
                        v_m, 1)
                    bump("v", f"ksv{iC}")
                if 0 <= iB < NTOT:
                    wv(p, "v", f"q{iB}")
                    wv(p, "s", f"trhe{iB}")
                    mm = None
                    for c in range(FCH):
                        bank = pb[2] if c < 4 else pb[3]
                        mm = p.matmul(bank[:, (c % 4) * 128:(c % 4) * 128 + 128],
                                      qlhs[pB_][:, bass.ts(c, 128)], idn[:],
                                      start=True, stop=True)
                    _ti(mm, p_m, 1)
                    bump("p", f"qT{iB}")
                if 0 <= iB < NTOT:
                    wv(v, "p", f"qT{iB}")
                    v.nd().tensor_copy(qlhs[pB_][:, 0:512], pb[2][:])
                    _ti(v.nd().tensor_copy(qlhs[pB_][:, 512:1024], pb[3][:]), v_m, 1)
                    bump("v", f"qTe{iB}")
                if 0 <= iA < NTOT:
                    wd(v, "x", f"x{iA}")
                    _ti(v.nd().reduce_sum(stat[pA][:, 4:5], xbuf[pA][:], AX.X), v_m, 1)
                    bump("v", f"lnr{iA}")
                    wv(s, "v", f"lnr{iA}")
                    _ti(s.activation(stat[pA][:, 0:1], stat[pA][:, 4:5], AFT.Copy,
                                     scale=1.0 / D), s_m, 1)
                    bump("s", f"lnm0{iA}")
                    wv(v, "s", f"lnm0{iA}")
                    _ti(v.nd().scalar_tensor_tensor(tpl[:], xbuf[pA][:], stat[pA][:, 0:1],
                                                    xbuf[pA][:], ALU.subtract, ALU.mult,
                                                    accum_out=stat[pA][:, 1:2]), v_m, 1)
                    bump("v", f"lnf{iA}")
                if 0 <= iD < NTOT:
                    wv(p, "s", f"trhe{iD}")
                    wv(p, "v", f"trl{iD}")
                    for r in range(4):
                        bank = (pb[0], pb[1], pb[4], pb[5])[r]
                        if iD >= 1 and r < 2:
                            wv(p, "s", f"pre{iD - 1}_{r}")
                        if r == 2:
                            wv(p, "v", f"trl{min(iD + 1, NTOT - 1)}")
                        half, b2 = (0, r) if r < 2 else (1, r - 2)
                        mm = None
                        for c in range(FCH):
                            for ps_ in range(3):
                                lhsT = (xTh[pD] if ps_ < 2 else xTl[pD])[:, bass.ts(c, 128)]
                                wsel = wa if ps_ != 1 else wb
                                base = c * 2048 + half * 1024 + b2 * 512
                                mm = p.matmul(bank[:], lhsT, wsel[:, base:base + 512],
                                              start=(c == 0 and ps_ == 0),
                                              stop=(c == FCH - 1 and ps_ == 2))
                        _ti(mm, p_m, 1)
                        bump("p", f"pr{iD}_{r}")
                        wv(s, "p", f"pr{iD}_{r}")
                        if r == 0 and iD >= 2:
                            wv(s, "p", f"kv{iD - 2}")
                        if r < 2:
                            ins = s.activation(kq[pD][:, 512 * r:512 * (r + 1)], bank[:], AFT.Copy)
                        else:
                            ins = s.activation(vex[pD][:, 512 * (r - 2):512 * (r - 1)], bank[:], AFT.Copy)
                        _ti(ins, s_m, 1)
                        bump("s", f"pre{iD}_{r}")
                if 0 <= iA < NTOT:
                    wv(s, "v", f"lnf{iA}")
                    s.activation(stat[pA][:, 2:3], stat[pA][:, 1:2], AFT.Ln,
                                 bias=csteps[:], scale=1.0 / D)
                    s.drain()
                    _ti(s.activation(stat[pA][:, 3:4], stat[pA][:, 2:3], AFT.Exp,
                                     scale=-0.5), s_m, 1)
                    bump("s", f"rstd{iA}")
                if 0 <= iB < NTOT:
                    wv(s, "v", f"trl{iB}")
                    s.drain()
                    dma_grp(s, "scr", [(scr_xh[iB], xTh[pB_][:]),
                                       (scr_xl[iB], xTl[pB_][:])], f"scrx{iB}")
                    wv(s, "v", f"qTe{iB}")
                    dma_grp(s, "scr", [(scr_ql[iB], qlhs[pB_][:])], f"scrq{iB}")
                if 0 <= iD < NTOT:
                    wv(g, "s", f"pre{iD}_1")
                    if iD >= 2:
                        wv(g, "v", f"feat{iD - 2}")
                    rope(g, "g", kq[pD], rot[pD], iD % NT, f"rope{iD}")
                if 0 <= iD < NTOT:
                    wv(s, "g", f"rope{iD}")
                    if iD >= 2:
                        wv(s, "v", f"feat{iD - 2}")
                    s.activation(ebuf[pD][:], rot[pD][:], AFT.Exp)
                    _ti(s.activation(rot[pD][:], rot[pD][:], AFT.Relu), s_m, 1)
                    bump("s", f"exp{iD}")
                if 0 <= iA < NTOT:
                    wv(v, "s", f"rstd{iA}")
                    if iA >= 2:
                        wv(v, "p", f"ptrl{iA - 2}")
                    v.nd().tensor_scalar(xnbuf[pA][:], xbuf[pA][:], stat[pA][:, 0:1],
                                         stat[pA][:, 3:4], ALU.subtract, ALU.mult)
                    v.reduce_max(stat[pA][:, 5:6], xnbuf[pA][:], AX.X, apply_absolute_value=True)
                    v.nd().tensor_copy(xhtok[pA][:], xnbuf[pA][:])
                    v.nd().tensor_scalar(ls_sb[:, iA:iA + 1], stat[pA][:, 5:6], 1e-12,
                                         1.0 / 127.0, ALU.max, ALU.mult)
                    v.reciprocal(stat[pA][:, 7:8], ls_sb[:, iA:iA + 1])
                    _ti(v.tensor_tensor(xltok[pA][:], xnbuf[pA][:], xhtok[pA][:], ALU.subtract),
                        v_m, 1)
                    bump("v", f"split{iA}")
                if 0 <= iA < NTOT:
                    if iA >= 2:
                        wd(v, "scr", f"scrq{iA - 2}")
                    v.drain()
                    v.nd().tensor_scalar(upl[:], xnbuf[pA][:], stat[pA][:, 7:8], 1.0, ALU.mult, ALU.mult)
                    rem = peel_core(v, upl[:])
                    v.nd().scalar_tensor_tensor(tpl[:], rem[:], -1.0, upl[:], ALU.mult, ALU.add)
                    _ti(v.nd().tensor_scalar(qlhs[pA][:], tpl[:], 1.0, 0.5, ALU.mult, ALU.add),
                        v_m, 1)
                    bump("v", f"q{iA}")
                if 0 <= iC < NTOT and (iC % NT) == NT - 1:
                    s_idx = iC // NT
                    stg = kvstg
                    wv(s, "p", f"kv{iC}")
                    s.activation(stg[0:64, 0:512], pb[6][0:64, 0:512], AFT.Copy)
                    _ti(s.activation(stg[0:64, 512:1024], pb[7][0:64, 0:512], AFT.Copy),
                        s_m, 1)
                    bump("s", f"kvstg{s_idx}")
                    wv(p, "s", f"kvstg{s_idx}")
                    wv(p, "v", f"ksv{iC}")
                    if 0 <= iB < NTOT:
                        wv(p, "v", f"qTe{iB}")
                    for pr in range(8):
                        h1 = 2 * pr + 1
                        mm = p.matmul(pb[2][:, pr * 64:(pr + 1) * 64], idsh[:],
                                      stg[0:64, h1 * 64:(h1 + 1) * 64],
                                      start=True, stop=True)
                    for pr in range(8):
                        mm = p.matmul(pb[3][:, pr:pr + 1],
                                      ks_acc[:, pr * 128:(pr + 1) * 128],
                                      ones_col[:, 0:1], start=True, stop=True)
                    _ti(mm, p_m, 1)
                    bump("p", f"kvsh{s_idx}")
                    wv(s, "p", f"kvsh{s_idx}")
                    ins = None
                    for pr in range(8):
                        base = (s_idx * 8 + pr) * 130
                        h0 = 2 * pr
                        c0 = h0 * 64
                        s.activation(kv_h[0:64, base:base + 64],
                                     stg[0:64, c0:c0 + 64], AFT.Copy)
                        s.activation(kv_h[64:128, base + 65:base + 129],
                                     pb[2][64:128, pr * 64:(pr + 1) * 64], AFT.Copy)
                        s.activation(kv_h[0:64, base + 64:base + 65],
                                     pb[3][0:64, pr:pr + 1], AFT.Copy)
                        ins = s.activation(kv_h[64:128, base + 129:base + 130],
                                           pb[3][64:128, pr:pr + 1], AFT.Copy)
                    _ti(ins, s_m, 1)
                    bump("s", f"kvh{s_idx}")
                    wv(v, "s", f"kvh{s_idx}")
                    ins = None
                    for pr in range(8):
                        base = (s_idx * 8 + pr) * 130
                        h0 = 2 * pr
                        c0 = h0 * 64
                        v.tensor_tensor(kv_l[0:64, base:base + 64],
                                        stg[0:64, c0:c0 + 64],
                                        kv_h[0:64, base:base + 64], ALU.subtract)
                        v.tensor_tensor(kv_l[64:128, base + 65:base + 129],
                                        pb[2][64:128, pr * 64:(pr + 1) * 64],
                                        kv_h[64:128, base + 65:base + 129], ALU.subtract)
                        v.tensor_tensor(kv_l[0:64, base + 64:base + 65],
                                        pb[3][0:64, pr:pr + 1],
                                        kv_h[0:64, base + 64:base + 65], ALU.subtract)
                        ins = v.tensor_tensor(kv_l[64:128, base + 129:base + 130],
                                              pb[3][64:128, pr:pr + 1],
                                              kv_h[64:128, base + 129:base + 130], ALU.subtract)
                    v.memset(ks_acc[:], 0.0)
                    _ti(ins, v_m, 1)
                    bump("v", f"kvs{s_idx}")

            for c in range(2 * FCH):
                src = (wqkvT_d[128 * c:128 * (c + 1), 0:D] if c < FCH
                       else woutT_d[128 * (c - FCH):128 * (c - FCH) + 128, :])
                dstc = c * 1024
                if c < 2:
                    wv(y, "p", f"pr{NTOT - 1}_3")
                    wv(y, "v", f"q{NTOT - 1}")
                else:
                    wv(y, "v", f"wb2_{c - 2}")
                dma_grp(y, "w", [(xbuf[c % 2][:], src)], f"w_b{c}")
                wd(s, "w", f"w_b{c}")
                _ti(s.activation(wa[:, dstc:dstc + 1024], xbuf[c % 2][:], AFT.Copy), s_m, 1)
                bump("s", f"wa2_{c}")
                wv(v, "s", f"wa2_{c}")
                _ti(v.tensor_tensor(wb[:, dstc:dstc + 1024], xbuf[c % 2][:],
                                    wa[:, dstc:dstc + 1024], ALU.subtract), v_m, 1)
                bump("v", f"wb2_{c}")

            for it in range(NTOT + 4):
                iL, iP, iM, iR, iF = it, it - 1, it - 2, it - 3, it - 4
                pL, pP, pM, pR, pF = iL % 2, iP % 2, iM % 2, iR % 2, iF % 2
                iX = it - 3
                pX = iX % 2
                if 0 <= iF < NTOT:
                    wd(p, "bx", f"bql{iF}")
                    wv(p, "v", f"resc{min(iF + 1, NTOT - 1)}")
                    if iF >= 1:
                        wv(p, "s", f"ffe{iF - 1}")
                    for r in range(2):
                        bank = pb[7] if r == 0 else pb[0]
                        mm = None
                        for c in range(FCH):
                            mm = p.matmul(bank[:], qlhs[pF][:, bass.ts(c, 128)],
                                          qrhs[:, c * 1024 + r * 512:c * 1024 + r * 512 + 512],
                                          start=(c == 0), stop=(c == FCH - 1))
                        if r == 1:
                            _ti(mm, p_m, 1)
                    bump("p", f"ffn{iF}")
                if 0 <= iF < NTOT:
                    wv(s, "p", f"ffn{iF}")
                    if iF >= 2:
                        wv(s, "v", f"epi{iF - 2}")
                    s.activation(ff_sb[pF][:, 0:512], pb[7][:], AFT.Copy)
                    _ti(s.activation(ff_sb[pF][:, 512:1024], pb[0][:], AFT.Copy), s_m, 1)
                    bump("s", f"ffe{iF}")
                if 0 <= iM < NTOT:
                    wv(p, "v", f"bfeat{iM}")
                    if iM >= 2:
                        wv(p, "s", f"aTh{iM - 2}")
                        wv(p, "v", f"aTl{iM - 2}")
                    mm = None
                    for c in range(FCH):
                        bank = pb[2] if c < 4 else pb[3]
                        mm = p.matmul(bank[:, (c % 4) * 128:(c % 4) * 128 + 128],
                                      kq[pM][:, bass.ts(c, 128)], idnf[:],
                                      start=True, stop=True)
                    _ti(mm, p_m, 1)
                    bump("p", f"qfT{iM}")
                if 0 <= iM < NTOT:
                    wv(s, "p", f"qfT{iM}")
                    if iM >= 2:
                        wv(s, "p", f"attn{iM - 2}")
                    s.activation(qfTh[pM][:, 0:512], pb[2][:], AFT.Copy)
                    _ti(s.activation(qfTh[pM][:, 512:1024], pb[3][:], AFT.Copy), s_m, 1)
                    bump("s", f"qfTh{iM}")
                if 0 <= iM < NTOT:
                    wv(v, "p", f"qfT{iM}")
                    wv(v, "s", f"qfTh{iM}")
                    if iM >= 2:
                        wv(v, "p", f"attn{iM - 2}")
                    v.tensor_tensor(qfTl[pM][:, 0:512], pb[2][:], qfTh[pM][:, 0:512], ALU.subtract)
                    _ti(v.tensor_tensor(qfTl[pM][:, 512:1024], pb[3][:], qfTh[pM][:, 512:1024],
                                        ALU.subtract), v_m, 1)
                    bump("v", f"qfTl{iM}")
                if 0 <= iP < NTOT:
                    wd(p, "bx", f"bxT{iP}")
                    if iP == 0:
                        wv(p, "v", f"wb2_{2 * FCH - 1}")
                    if iP >= 4:
                        wv(p, "s", f"woe{iP - 4}")
                    if iP >= 1:
                        wv(p, "v", f"bevq{iP - 1}")
                    for r in range(2):
                        bank = pb[5 + r]
                        mm = None
                        for c in range(FCH):
                            for ps_ in range(3):
                                lhsT = (xTh[pP] if ps_ < 2 else xTl[pP])[:, bass.ts(c, 128)]
                                wsel = wa if ps_ != 1 else wb
                                base = c * 1024 + r * 512
                                mm = p.matmul(bank[:], lhsT, wsel[:, base:base + 512],
                                              start=(c == 0 and ps_ == 0),
                                              stop=(c == FCH - 1 and ps_ == 2))
                        _ti(mm, p_m, 1)
                        bump("p", f"bpr{iP}_{r}")
                if 0 <= iP < NTOT:
                    if iP >= 2:
                        wv(v, "p", f"qfT{iP - 2}")
                    for r in range(2):
                        wv(v, "p", f"bpr{iP}_{r}")
                        _ti(v.nd().tensor_copy(kq[pP][:, 512 * r:512 * (r + 1)], pb[5 + r][:]), v_m, 1)
                        bump("v", f"bevq{iP}" if r == 1 else f"bevq{iP}_0")
                if 0 <= iP < NTOT:
                    wv(g, "v", f"bevq{iP}")
                    if iP >= 2:
                        wv(g, "v", f"bfeat{iP - 2}")
                    rope(g, "g", kq[pP], rot[pP], iP % NT, f"brope{iP}")
                if 0 <= iP < NTOT:
                    wv(s, "g", f"brope{iP}")
                    if iP >= 2:
                        wv(s, "v", f"bfeat{iP - 2}")
                    s.activation(ebuf[pP][:], rot[pP][:], AFT.Exp, scale=SCALE_Q)
                    _ti(s.activation(rot[pP][:], rot[pP][:], AFT.Relu, scale=SCALE_Q),
                        s_m, 1)
                    bump("s", f"bexp{iP}")
                if 0 <= iM < NTOT:
                    sM = iM // NT
                    wv(p, "v", f"qfTl{iM}")
                    wv(p, "s", f"qfTh{iM}")
                    if 0 <= iF < NTOT:
                        wv(p, "s", f"ffe{iF}")
                    if iM == 0:
                        wv(p, "v", f"kvs{n_seq - 1}")
                    if iM >= 1:
                        wv(p, "v", f"resc{iM - 1}")
                    mm = None
                    for pr in range(8):
                        bank = pb[0] if pr < 3 else (pb[1] if pr < 6 else pb[4])
                        off = (pr % 3) * 130 if pr < 6 else (pr - 6) * 130
                        kvbase = (sM * 8 + pr) * 130
                        for ps_ in range(3):
                            lhsT = (qfTh[pM] if ps_ < 2 else qfTl[pM])[:, bass.ts(pr, 128)]
                            rhs_ = (kv_h if ps_ != 1 else kv_l)[:, kvbase:kvbase + 130]
                            mm = p.matmul(bank[:, off:off + 130], lhsT, rhs_,
                                          start=(ps_ == 0 and pr in (0, 3, 6)),
                                          stop=(ps_ == 2 and pr in (2, 5, 7)))
                    _ti(mm, p_m, 1)
                    bump("p", f"attn{iM}")
                if 0 <= iM < NTOT:
                    wv(v, "p", f"attn{iM}")
                    if iM >= 2:
                        wv(v, "p", f"aT{iM - 2}")
                    dn = stat[pM][:, 8:24]
                    rc = stat[pM][:, 24:40]
                    v.nd().tensor_single_scalar(dn[:, 0:6].rearrange("p (j t) -> p j t", t=2),
                                           bass.AP(pb[0][:].tensor, pb[0][:].offset + 64,
                                                   [list(pb[0][:].ap[0]), [130, 3], [65, 2]]),
                                           1e-6, ALU.max)
                    v.nd().tensor_single_scalar(dn[:, 6:12].rearrange("p (j t) -> p j t", t=2),
                                           bass.AP(pb[1][:].tensor, pb[1][:].offset + 64,
                                                   [list(pb[1][:].ap[0]), [130, 3], [65, 2]]),
                                           1e-6, ALU.max)
                    v.nd().tensor_single_scalar(dn[:, 12:16].rearrange("p (j t) -> p j t", t=2),
                                           bass.AP(pb[4][:].tensor, pb[4][:].offset + 64,
                                                   [list(pb[4][:].ap[0]), [130, 2], [65, 2]]),
                                           1e-6, ALU.max)
                    v.reciprocal(rc[:], dn[:])
                    v.drain()
                    ins = None
                    for bi, (bank, npr, dbase) in enumerate(((pb[0], 3, 0), (pb[1], 3, 3), (pb[4], 2, 6))):
                        num = bass.AP(bank[:].tensor, bank[:].offset,
                                      [list(bank[:].ap[0]), [130, npr], [65, 2], [1, 64]])
                        rcb = bass.AP(rc.tensor, rc.offset + dbase * 2,
                                      [list(rc.ap[0]), [2, npr], [1, 2], [0, 64]])
                        dst = xnbuf[pM][:]
                        dst2 = bass.AP(dst.tensor, dst.offset + dbase * 2 * 64,
                                       [list(dst.ap[0]), [128, npr], [64, 2], [1, 64]])
                        ins = v.tensor_tensor(dst2, num, rcb, ALU.mult)
                    _ti(ins, v_m, 1)
                    bump("v", f"resc{iM}")
                if 0 <= iF < NTOT:
                    wv(p, "s", f"aTh{iF}")
                    wv(p, "v", f"aTl{iF}")
                    wv(p, "v", f"bevq{min(max(iP, 0), NTOT - 1)}")
                    for r in range(2):
                        bank = pb[5 + r]
                        mm = None
                        for c in range(FCH):
                            for ps_ in range(3):
                                lhsT = (xhtok[pF] if ps_ < 2 else xltok[pF])[:, bass.ts(c, 128)]
                                wsel = wa if ps_ != 1 else wb
                                base = 8192 + c * 1024 + r * 512
                                mm = p.matmul(bank[:], lhsT, wsel[:, base:base + 512],
                                              start=(c == 0 and ps_ == 0),
                                              stop=(c == FCH - 1 and ps_ == 2))
                        if r == 1:
                            _ti(mm, p_m, 1)
                    bump("p", f"wout{iF}")
                if 0 <= iF < NTOT:
                    wv(s, "p", f"wout{iF}")
                    if iF >= 2:
                        wv(s, "v", f"epi{iF - 2}")
                    s.activation(vex[pF][:, 0:512], pb[5][:], AFT.Copy)
                    _ti(s.activation(vex[pF][:, 512:1024], pb[6][:], AFT.Copy), s_m, 1)
                    bump("s", f"woe{iF}")
                if 0 <= iR < NTOT:
                    wv(p, "v", f"resc{iR}")
                    if 0 <= iM < NTOT:
                        wv(p, "s", f"qfTh{iM}")
                        wv(p, "v", f"qfTl{iM}")
                    elif iR >= 1:
                        wv(p, "s", f"aTh{iR - 1}")
                        wv(p, "v", f"aTl{iR - 1}")
                    mm = None
                    for c in range(FCH):
                        bank = pb[2] if c < 4 else pb[3]
                        mm = p.matmul(bank[:, (c % 4) * 128:(c % 4) * 128 + 128],
                                      xnbuf[pR][:, bass.ts(c, 128)], idnf[:],
                                      start=True, stop=True)
                    _ti(mm, p_m, 1)
                    bump("p", f"aT{iR}")
                if 0 <= iR < NTOT:
                    wv(s, "p", f"aT{iR}")
                    if iR >= 2:
                        wv(s, "p", f"wout{iR - 2}")
                    s.activation(xhtok[pR][:, 0:512], pb[2][:], AFT.Copy)
                    _ti(s.activation(xhtok[pR][:, 512:1024], pb[3][:], AFT.Copy), s_m, 1)
                    bump("s", f"aTh{iR}")
                if 0 <= iR < NTOT:
                    wv(v, "p", f"aT{iR}")
                    wv(v, "s", f"aTh{iR}")
                    v.tensor_tensor(xltok[pR][:, 0:512], pb[2][:], xhtok[pR][:, 0:512], ALU.subtract)
                    _ti(v.tensor_tensor(xltok[pR][:, 512:1024], pb[3][:], xhtok[pR][:, 512:1024],
                                        ALU.subtract), v_m, 1)
                    bump("v", f"aTl{iR}")
                if 0 <= iP < NTOT:
                    wv(v, "s", f"bexp{iP}")
                    _ti(v.nd().scalar_tensor_tensor(kq[pP][:], ebuf[pP][:], 1.0, rot[pP][:],
                                                    ALU.min, ALU.add), v_m, 1)
                    bump("v", f"bfeat{iP}")
                if 0 <= iF < NTOT:
                    wv(v, "s", f"woe{iF}")
                    wv(v, "s", f"ffe{iF}")
                    wd(v, "bx", f"bxx{iF}")
                    if iF >= 2:
                        wd(v, "out", f"out{iF - 2}")
                    v.nd().scalar_tensor_tensor(tpl[:], ff_sb[pF][:], ls_sb[:, iF:iF + 1],
                                                rs_bc[:], ALU.mult, ALU.mult)
                    v.tensor_tensor(outb[pF][:], xbuf[pF][:], vex[pF][:, 0:D], ALU.add)
                    _ti(v.tensor_tensor(outb[pF][:], outb[pF][:], tpl[:], ALU.add), v_m, 1)
                    bump("v", f"epi{iF}")
                if 0 <= iF < NTOT:
                    wv(s, "v", f"epi{iF}")
                    dma_grp(s, "out", [(out_d[TILE * iF:TILE * (iF + 1), :], outb[pF][:])],
                            f"out{iF}")
                if 0 <= iL < NTOT:
                    if iL >= 2:
                        wv(y, "p", f"bpr{iL - 2}_1")
                    else:
                        lastp = NTOT - 1 - ((NTOT - 1 - iL) % 2)
                        wv(y, "p", f"pr{lastp}_3")
                        wd(y, "scr", f"scrx{lastp}")
                    dma_grp(y, "bx", [(xTh[pL][:], scr_xh[iL]),
                                      (xTl[pL][:], scr_xl[iL])], f"bxT{iL}")
                if 0 <= iX < NTOT:
                    if iX >= 2:
                        wv(y, "v", f"epi{iX - 2}")
                        wv(y, "p", f"ffn{iX - 2}")
                    else:
                        lastp = NTOT - 1 - ((NTOT - 1 - iX) % 2)
                        wv(y, "v", f"wb2_{2 * FCH - 2 + iX}")
                        wd(y, "scr", f"scrq{lastp}")
                    dma_grp(y, "bx", [(xbuf[pX][:], x_d[TILE * iX:TILE * (iX + 1), :])], f"bxx{iX}")
                    dma_grp(y, "bx", [(qlhs[pX][:], scr_ql[iX])], f"bql{iX}")
            y.wait_ge(s_out, DM["out"])
            return C, DM

        @block.sync
        def _(sync):
            walk("y", sync)

        @block.vector
        def _(vector):
            walk("v", vector)

        @block.scalar
        def _(scalar):
            walk("s", scalar)

        @block.tensor
        def _(tensor):
            walk("p", tensor)

        @block.gpsimd
        def _(gpsimd):
            walk("g", gpsimd)

    return nc


def kernel(L0_attn_x, L0_attn_w_qkv, L0_attn_w_out, L0_ffn_rhs, _nt=T // TILE):
    x = np.ascontiguousarray(np.asarray(L0_attn_x, dtype=np.float32))
    wqkvT = np.ascontiguousarray(np.asarray(L0_attn_w_qkv, dtype=np.float32).T)
    woutT = np.ascontiguousarray(np.asarray(L0_attn_w_out, dtype=np.float32).T)
    rhs = np.ascontiguousarray(np.asarray(L0_ffn_rhs, dtype=np.float32))
    rhsT = np.ascontiguousarray(rhs.T)
    cos, sin = _host_tables(_nt)
    idn = np.eye(128, dtype=ml_dtypes.bfloat16)
    idnf = np.eye(128, dtype=np.float32)
    idsh = np.eye(64, 128, k=64, dtype=np.float32)
    nb, tt, dd = x.shape
    n_seq = nb // NCORES
    nc = build_nc(NT=_nt, n_seq=n_seq)
    in_maps = []
    for c in range(NCORES):
        xc = np.ascontiguousarray(x[c * n_seq:(c + 1) * n_seq].reshape(n_seq * tt, dd))
        in_maps.append({"x": xc, "wqkvT": wqkvT, "woutT": woutT, "rhs": rhs,
                        "rhsT": rhsT, "cos": cos, "sin": sin, "idn": idn,
                        "idnf": idnf, "idsh": idsh})
    res = run_bass_kernel_spmd(nc, in_maps, core_ids=list(range(NCORES)))
    out = np.concatenate([r["out"].reshape(n_seq, tt, dd) for r in res.results], axis=0)
    return out


# version 17
# speedup vs baseline: 1.1860x; 1.1860x over previous
import contextlib
import os

import ml_dtypes
import numpy as np

import concourse.bass as bass
import concourse.mybir as mybir
from concourse.bass_utils import run_bass_kernel_spmd

F32 = mybir.dt.float32
BF16 = mybir.dt.bfloat16
ALU = mybir.AluOpType
AFT = mybir.ActivationFunctionType
AX = mybir.AxisListType

B, T, D = 16, 4096, 1024
H, HD = 16, 64
NCORES = 8
TILE = 128
FCH = D // 128
SCALE_Q = float(HD) ** -0.5
LN_EPS = 1e-5
TRACE_SCHED = None


def _host_tables(nt):
    import jax
    import jax.numpy as jnp
    with jax.default_device(jax.local_devices(backend="cpu")[0]):
        inv = 1.0 / 10000.0 ** (jnp.arange(0, HD, 2, dtype=jnp.float32) / HD)
        t = jnp.arange(nt * TILE, dtype=jnp.float32)
        freqs = t[:, None] * inv[None, :]
        cos = np.asarray(jnp.cos(freqs), dtype=np.float32)
        sin = np.asarray(jnp.sin(freqs), dtype=np.float32)

    def fold(a):
        a = a.reshape(nt, TILE, 32)
        return np.ascontiguousarray(a.transpose(1, 0, 2).reshape(TILE, nt * 32))
    return fold(cos), fold(sin)


_DRAIN_ALL = (
    "tensor_copy", "tensor_tensor", "tensor_scalar", "tensor_single_scalar",
    "scalar_tensor_tensor", "tensor_add", "tensor_sub", "tensor_mul",
    "tensor_max", "reduce_max", "reduce_sum", "tensor_reduce", "reciprocal",
    "memset", "activation", "dma_start", "select",
)
_DRAIN_SETS = {
    "all": frozenset(_DRAIN_ALL),
    "dma": frozenset(("dma_start",)),
    "dma_ts": frozenset(("dma_start", "tensor_scalar", "tensor_single_scalar",
                         "scalar_tensor_tensor", "activation")),
    "x1": frozenset(("dma_start", "tensor_scalar", "tensor_single_scalar",
                     "scalar_tensor_tensor")),
    "x2": frozenset(("dma_start", "activation")),
    "none": frozenset(()),
}
_DRAIN_SETS["lean"] = frozenset((
    "dma_start", "memset", "tensor_copy", "reduce_max", "reduce_sum",
    "tensor_reduce", "reciprocal", "select",
))
_DRAIN_SETS["lean_tt"] = _DRAIN_SETS["lean"] | frozenset(("tensor_tensor",))
_DRAIN_SETS["lean_act"] = _DRAIN_SETS["lean"] | frozenset(("activation",))
_DRAIN_SETS["lean_tt_act"] = _DRAIN_SETS["lean_tt"] | frozenset(("activation",))
_DRAIN_SETS["lean_sst"] = _DRAIN_SETS["lean"] | frozenset(
    ("scalar_tensor_tensor", "tensor_scalar", "tensor_single_scalar"))
_IS_SIM = (os.environ.get("BASS_SIM_FP64") == "1"
           or os.environ.get("K_SIMCHECK") == "1")
_DRAIN_DEFAULT = "all" if _IS_SIM else "lean_sst"
_DRAIN_OPS = _DRAIN_SETS.get(os.environ.get("K_DRAINS", _DRAIN_DEFAULT),
                             _DRAIN_SETS[_DRAIN_DEFAULT])


class _Eng:

    def __init__(self, real, auto_drain=False, sim_drain=False):
        self._r = real
        self._ad = auto_drain
        self._sd = sim_drain
        self._nd = None

    def nd(self):
        if self._nd is None:
            self._nd = _Eng(self._r, auto_drain=self._sd)
        return self._nd

    def __getattr__(self, n):
        if self._r is None:
            return lambda *a, **k: None
        f = getattr(self._r, n)
        if self._ad and n in _DRAIN_OPS:
            r = self._r

            def wrapped(*a, **k):
                r.drain()
                return f(*a, **k)
            return wrapped
        return f


def _ti(ins, semh, n=1):
    if ins is not None:
        ins.then_inc(semh, n)


def build_nc(NT=T // TILE, n_seq=2):
    NTOT = NT * n_seq
    TOK = NTOT * TILE
    nc = bass.Bass("TRN2", target_bir_lowering=False, debug=False)

    x_d = nc.dram_tensor("x", [TOK, D], F32, kind="ExternalInput").ap()
    wqkvT_d = nc.dram_tensor("wqkvT", [D, 3 * D], F32, kind="ExternalInput").ap()
    woutT_d = nc.dram_tensor("woutT", [D, D], F32, kind="ExternalInput").ap()
    rhs_d = nc.dram_tensor("rhs", [D, D], F32, kind="ExternalInput").ap()
    rhsT_d = nc.dram_tensor("rhsT", [D, D], F32, kind="ExternalInput").ap()
    cos_d = nc.dram_tensor("cos", [TILE, NT * 32], F32, kind="ExternalInput").ap()
    sin_d = nc.dram_tensor("sin", [TILE, NT * 32], F32, kind="ExternalInput").ap()
    idn_d = nc.dram_tensor("idn", [128, 128], BF16, kind="ExternalInput").ap()
    idnf_d = nc.dram_tensor("idnf", [128, 128], F32, kind="ExternalInput").ap()
    idsh_d = nc.dram_tensor("idsh", [64, 128], F32, kind="ExternalInput").ap()
    out_d = nc.dram_tensor("out", [TOK, D], F32, kind="ExternalOutput").ap()
    scr_xh = nc.dram_tensor("scr_xh", [NTOT, TILE, D], BF16, kind="Internal").ap()
    scr_xl = nc.dram_tensor("scr_xl", [NTOT, TILE, D], BF16, kind="Internal").ap()
    scr_ql = nc.dram_tensor("scr_ql", [NTOT, TILE, D], BF16, kind="Internal").ap()

    st_ = contextlib.ExitStack()
    with st_:
        def sb(name, shape, dt=F32):
            return st_.enter_context(nc.sbuf_tensor(name, shape, dt))

        wa = sb("wa", [128, FCH * 2048], BF16)
        wb = sb("wb", [128, FCH * 2048], BF16)
        qrhs = sb("qrhs", [128, FCH * D], BF16)
        cos_sb = sb("cos_sb", [TILE, NT * 32])
        sin_sb = sb("sin_sb", [TILE, NT * 32])
        idn = sb("idn_s", [128, 128], BF16)
        idnf = sb("idnf_s", [128, 128])
        idsh = sb("idsh_s", [64, 128])
        ones_col = sb("ones_col", [128, 1])
        ks_acc = sb("ks_acc", [128, D])
        kvstg = sb("kvstg", [64, 1024])
        ones1 = sb("ones1", [1, 128])
        rs_bc = sb("rs_bc", [128, D])
        kv_h = sb("kv_h", [128, n_seq * 8 * 130], BF16)
        kv_l = sb("kv_l", [128, n_seq * 8 * 130], BF16)
        ls_sb = sb("ls_sb", [128, NTOT])
        xbuf = [sb(f"xbuf{p}", [TILE, D]) for p in range(2)]
        xnbuf = [sb(f"xn{p}", [TILE, D]) for p in range(2)]
        xhtok = [sb(f"xht{p}", [TILE, D], BF16) for p in range(2)]
        xltok = [sb(f"xlt{p}", [TILE, D], BF16) for p in range(2)]
        xTh = [sb(f"xTh{p}", [128, D], BF16) for p in range(2)]
        xTl = [sb(f"xTl{p}", [128, D], BF16) for p in range(2)]
        kq = [sb(f"kq{p}", [TILE, D]) for p in range(2)]
        rot = [sb(f"rot{p}", [TILE, D]) for p in range(2)]
        vex = [sb(f"vex{p}", [TILE, 16 * 65]) for p in range(2)]
        ebuf = [sb(f"e{p}", [TILE, D]) for p in range(2)]
        qlhs = [sb(f"qlhs{p}", [TILE, D], BF16) for p in range(2)]
        stat = [sb(f"stat{p}", [TILE, 40]) for p in range(2)]
        qfTh_s = sb("qfTh_s", [128, D], BF16)
        qfTl_s = sb("qfTl_s", [128, D], BF16)
        qfTh = [qfTh_s, qfTh_s]
        qfTl = [qfTl_s, qfTl_s]
        upl = sb("upl", [TILE, D])
        vpl = sb("vpl", [TILE, D])
        vpl2 = sb("vpl2", [TILE, D])
        cpl = sb("cpl", [TILE, D])
        tpl = sb("tpl", [TILE, D])
        gtmp = sb("gtmp", [TILE, 512])
        csteps = sb("csteps", [TILE, 1])
        ff_sb = [upl, vpl]
        outb = [vpl2, cpl]
        rr_bc = xnbuf[0]

        pb = [st_.enter_context(nc.psum_tensor(f"pb{j}", [128, 512], F32)) for j in range(8)]

        s_x = st_.enter_context(nc.semaphore("s_x"))
        s_w = st_.enter_context(nc.semaphore("s_w"))
        s_scr = st_.enter_context(nc.semaphore("s_scr"))
        s_bx = st_.enter_context(nc.semaphore("s_bx"))
        s_out = st_.enter_context(nc.semaphore("s_out"))
        v_m = st_.enter_context(nc.semaphore("v_m"))
        s_m = st_.enter_context(nc.semaphore("s_m"))
        p_m = st_.enter_context(nc.semaphore("p_m"))
        g_m = st_.enter_context(nc.semaphore("g_m"))

        block = st_.enter_context(nc.Block())

        _ad = (_IS_SIM or os.environ.get("K_DRAINS", _DRAIN_DEFAULT) != "native")

        _simd = _IS_SIM

        def walk(eng, real):
            v = _Eng(real if eng == "v" else None, auto_drain=_ad, sim_drain=_simd)
            s = _Eng(real if eng == "s" else None, auto_drain=_ad, sim_drain=_simd)
            p = _Eng(real if eng == "p" else None)
            g = _Eng(real if eng == "g" else None, auto_drain=_ad, sim_drain=_simd)
            y = _Eng(real if eng == "y" else None)
            C = {"v": 0, "s": 0, "p": 0, "g": 0}
            DM = {"x": 0, "w": 0, "scr": 0, "bx": 0, "out": 0}
            SEMH = {"v": v_m, "s": s_m, "p": p_m, "g": g_m}
            DSEM = {"x": s_x, "w": s_w, "scr": s_scr, "bx": s_bx, "out": s_out}
            M = {}

            def bump(k, name=None):
                C[k] += 1
                if TRACE_SCHED is not None and eng == k:
                    TRACE_SCHED.append(("bump", eng, k, C[k], name))
                if name is not None:
                    M[name] = C[k]
                return C[k]

            def dbump(k, name=None):
                DM[k] += 16
                if name is not None:
                    M[name] = DM[k]
                return DM[k]

            def wv(e, k, name):
                if TRACE_SCHED is not None and e._r is not None:
                    TRACE_SCHED.append(("wait", eng, k, M[name], name))
                e.wait_ge(SEMH[k], M[name])

            def wd(e, k, name):
                e.wait_ge(DSEM[k], M[name])

            def dma_grp(e, k, transfers, name=None):
                e.wait_ge(DSEM[k], DM[k])
                for out_ap, in_ap in transfers:
                    _ti(e.dma_start(out=out_ap, in_=in_ap), DSEM[k], 16)
                    DM[k] += 16
                if name is not None:
                    M[name] = DM[k]

            def rope(e, ek, src, dst, jt, mark):
                co = cos_sb[:, 32 * jt:32 * jt + 32]
                si = sin_sb[:, 32 * jt:32 * jt + 32]
                cob = bass.AP(co.tensor, co.offset, [list(co.ap[0]), [0, H], list(co.ap[1])])
                sib = bass.AP(si.tensor, si.offset, [list(si.ap[0]), [0, H], list(si.ap[1])])
                sv = src[:].rearrange("p (h two d) -> p h two d", two=2, d=32)
                dv = dst[:].rearrange("p (h two d) -> p h two d", two=2, d=32)
                gv = gtmp[:].rearrange("p (h d) -> p h d", d=32)
                se, so = sv[:, :, 0, :], sv[:, :, 1, :]
                de, do = dv[:, :, 0, :], dv[:, :, 1, :]
                ge, go = gv, gv
                e.tensor_tensor(de, se, cob, ALU.mult)
                e.tensor_tensor(ge, so, sib, ALU.mult)
                e.tensor_tensor(de, de, ge, ALU.subtract)
                e.tensor_tensor(do, se, sib, ALU.mult)
                e.tensor_tensor(go, so, cob, ALU.mult)
                _ti(e.tensor_tensor(do, do, go, ALU.add), SEMH[ek], 1)
                bump(ek, mark)

            def peel_core(e, y_ap):
                e.tensor_scalar(vpl[:], y_ap, 1.0, 128.5, ALU.mult, ALU.add)
                src_, dst_ = vpl, vpl2
                bit = 128.0
                en = e.nd()
                while bit >= 1.0:
                    en.tensor_single_scalar(cpl[:], src_[:], bit, ALU.is_ge)
                    en.scalar_tensor_tensor(dst_[:], cpl[:], -bit, src_[:], ALU.mult, ALU.add)
                    src_, dst_ = dst_, src_
                    bit /= 2.0
                return src_

            dma_grp(y, "w", [(cos_sb[:], cos_d), (sin_sb[:], sin_d),
                             (idn[:], idn_d), (idnf[:], idnf_d),
                             (idsh[:], idsh_d)], "w_consts")

            v.memset(ones1[:], 1.0)
            v.memset(ones_col[:], 1.0)
            v.memset(ks_acc[:], 0.0)
            v.memset(kv_h[:], 0.0)
            v.memset(csteps[:], LN_EPS)
            _ti(v.memset(kv_l[:], 0.0), v_m, 1)
            bump("v", "v_init")

            for c in range(FCH):
                if c >= 2:
                    wv(y, "v", f"wkv_lo{c - 2}")
                base = 128 * c
                dma_grp(y, "w", [(xbuf[c % 2][:], wqkvT_d[base:base + 128, D:2 * D]),
                                 (xnbuf[c % 2][:], wqkvT_d[base:base + 128, 2 * D:3 * D])],
                        f"w_kv{c}")
                wd(s, "w", f"w_kv{c}")
                s.activation(wa[:, c * 2048:c * 2048 + 1024], xbuf[c % 2][:], AFT.Copy)
                _ti(s.activation(wa[:, c * 2048 + 1024:c * 2048 + 2048], xnbuf[c % 2][:], AFT.Copy),
                    s_m, 1)
                bump("s", f"wkv_hi{c}")
                wv(v, "s", f"wkv_hi{c}")
                v.tensor_tensor(wb[:, c * 2048:c * 2048 + 1024], xbuf[c % 2][:],
                                wa[:, c * 2048:c * 2048 + 1024], ALU.subtract)
                _ti(v.tensor_tensor(wb[:, c * 2048 + 1024:c * 2048 + 2048], xnbuf[c % 2][:],
                                    wa[:, c * 2048 + 1024:c * 2048 + 2048], ALU.subtract),
                    v_m, 1)
                bump("v", f"wkv_lo{c}")

            for c in range(FCH):
                if c >= 2:
                    wv(y, "v", f"ramax{c - 2}")
                dma_grp(y, "w", [(ebuf[c % 2][:], rhsT_d[128 * c:128 * (c + 1), :])], f"w_rT{c}")
                wd(v, "w", f"w_rT{c}")
                _ti(v.reduce_max(stat[0][:, 8 + c:9 + c], ebuf[c % 2][:], AX.X,
                                 apply_absolute_value=True), v_m, 1)
                bump("v", f"ramax{c}")
            v.drain()
            v.tensor_single_scalar(stat[0][:, 16:24], stat[0][:, 8:16], 1e-12, ALU.max)
            v.tensor_single_scalar(stat[1][:, 8:16], stat[0][:, 16:24], 1.0 / 127.0, ALU.mult)
            v.reciprocal(stat[1][:, 16:24], stat[0][:, 16:24])
            v.drain()
            _ti(v.tensor_single_scalar(stat[0][:, 0:8], stat[1][:, 16:24], 127.0, ALU.mult),
                v_m, 1)
            bump("v", "rsc")
            wv(p, "v", "rsc")
            wd(p, "w", "w_consts")
            for r in range(FCH):
                bank = pb[0] if r < 4 else pb[2]
                p.matmul(bank[0:1, 128 * (r % 4):128 * (r % 4) + 128], stat[1][:, 8 + r:9 + r],
                         idnf[:], start=True, stop=True)
            mm = None
            for r in range(FCH):
                bank = pb[1] if r < 4 else pb[3]
                mm = p.matmul(bank[0:1, 128 * (r % 4):128 * (r % 4) + 128], stat[0][:, 0 + r:1 + r],
                              idnf[:], start=True, stop=True)
            _ti(mm, p_m, 1)
            bump("p", "rowsT")
            wv(v, "p", "rowsT")
            v.nd().tensor_copy(upl[0:1, 0:512], pb[0][0:1, 0:512])
            v.nd().tensor_copy(upl[0:1, 512:1024], pb[2][0:1, 0:512])
            v.nd().tensor_copy(vpl[0:1, 0:512], pb[1][0:1, 0:512])
            _ti(v.nd().tensor_copy(vpl[0:1, 512:1024], pb[3][0:1, 0:512]), v_m, 1)
            bump("v", "rows_ev")
            wv(p, "v", "rows_ev")
            p.matmul(pb[2][:], ones1[:], upl[0:1, 0:512], start=True, stop=True)
            p.matmul(pb[3][:], ones1[:], upl[0:1, 512:1024], start=True, stop=True)
            p.matmul(pb[4][:], ones1[:], vpl[0:1, 0:512], start=True, stop=True)
            _ti(p.matmul(pb[5][:], ones1[:], vpl[0:1, 512:1024], start=True, stop=True),
                p_m, 1)
            bump("p", "rows_bc")
            wv(v, "p", "rows_bc")
            v.nd().tensor_copy(rs_bc[:, 0:512], pb[2][:])
            v.nd().tensor_copy(rs_bc[:, 512:1024], pb[3][:])
            v.nd().tensor_copy(rr_bc[:, 0:512], pb[4][:])
            _ti(v.nd().tensor_copy(rr_bc[:, 512:1024], pb[5][:]), v_m, 1)
            bump("v", "rsbc")

            for c in range(FCH):
                if c >= 2:
                    wv(y, "v", f"qrhs{c - 2}")
                else:
                    wv(y, "v", "rsbc")
                dma_grp(y, "w", [(ebuf[c % 2][:], rhs_d[128 * c:128 * (c + 1), :])], f"w_r{c}")
                wd(v, "w", f"w_r{c}")
                v.tensor_tensor(upl[:], ebuf[c % 2][:], rr_bc[:], ALU.mult)
                rem = peel_core(v, upl[:])
                v.nd().scalar_tensor_tensor(tpl[:], rem[:], -1.0, upl[:], ALU.mult, ALU.add)
                _ti(v.nd().tensor_scalar(qrhs[:, bass.ts(c, D)], tpl[:], 1.0, 0.5, ALU.mult, ALU.add),
                    v_m, 1)
                bump("v", f"qrhs{c}")

            for it in range(NTOT + 2):
                iA, iB, iC = it, it - 1, it - 2
                pA, pB_, pC = iA % 2, iB % 2, iC % 2
                if 0 <= iA < NTOT:
                    if iA >= 2:
                        wv(y, "v", f"q{iA - 2}")
                    dma_grp(y, "x", [(xbuf[pA][:], x_d[TILE * iA:TILE * (iA + 1), :])], f"x{iA}")
                if 0 <= iB < NTOT:
                    wv(p, "v", f"split{iB}")
                    if iB >= 1:
                        wv(p, "v", f"qTe{iB - 1}")
                        wv(p, "v", f"trl{iB - 1}")
                        wv(p, "s", f"pre{iB - 1}_2")
                    if iB >= 1 and (iB - 1) % NT == 0 and iB > 1:
                        wv(p, "v", f"kvs{(iB - 1) // NT - 1}")
                    mm = None
                    for c in range(FCH):
                        bank = pb[2] if c < 4 else pb[3]
                        mm = p.matmul(bank[:, (c % 4) * 128:(c % 4) * 128 + 128],
                                      xhtok[pB_][:, bass.ts(c, 128)], idn[:],
                                      start=True, stop=True)
                    _ti(mm, p_m, 1)
                    bump("p", f"trh{iB}")
                    if iB >= 1:
                        wv(p, "s", f"pre{iB - 1}_3")
                    mm = None
                    for c in range(FCH):
                        bank = pb[4] if c < 4 else pb[5]
                        mm = p.matmul(bank[:, (c % 4) * 128:(c % 4) * 128 + 128],
                                      xltok[pB_][:, bass.ts(c, 128)], idn[:],
                                      start=True, stop=True)
                    _ti(mm, p_m, 1)
                    bump("p", f"ptrl{iB}")
                if 0 <= iB < NTOT:
                    wv(v, "p", f"ptrl{iB}")
                    if iB >= 2:
                        wd(v, "scr", f"scrx{iB - 2}")
                    v.nd().tensor_copy(xTl[pB_][:, 0:512], pb[4][:])
                    _ti(v.nd().tensor_copy(xTl[pB_][:, 512:1024], pb[5][:]), v_m, 1)
                    bump("v", f"trl{iB}")
                if 0 <= iB < NTOT:
                    wv(s, "p", f"trh{iB}")
                    if iB >= 2:
                        wd(s, "scr", f"scrx{iB - 2}")
                    s.activation(xTh[pB_][:, 0:512], pb[2][:], AFT.Copy)
                    _ti(s.activation(xTh[pB_][:, 512:1024], pb[3][:], AFT.Copy), s_m, 1)
                    bump("s", f"trhe{iB}")
                if 0 <= iC < NTOT:
                    wv(v, "s", f"exp{iC}")
                    _ti(v.nd().scalar_tensor_tensor(kq[pC][:], ebuf[pC][:], 1.0, rot[pC][:],
                                                    ALU.min, ALU.add), v_m, 1)
                    bump("v", f"feat{iC}")
                if 0 <= iA < NTOT:
                    wd(v, "x", f"x{iA}")
                    _ti(v.nd().reduce_sum(stat[pA][:, 4:5], xbuf[pA][:], AX.X), v_m, 1)
                    bump("v", f"lnr{iA}")
                    wv(s, "v", f"lnr{iA}")
                    _ti(s.activation(stat[pA][:, 0:1], stat[pA][:, 4:5], AFT.Copy,
                                     scale=1.0 / D), s_m, 1)
                    bump("s", f"lnm0{iA}")
                    wv(v, "s", f"lnm0{iA}")
                    _ti(v.nd().scalar_tensor_tensor(tpl[:], xbuf[pA][:], stat[pA][:, 0:1],
                                                    xbuf[pA][:], ALU.subtract, ALU.mult,
                                                    accum_out=stat[pA][:, 1:2]), v_m, 1)
                    bump("v", f"lnf{iA}")
                if 0 <= iB < NTOT:
                    wv(s, "v", f"trl{iB}")
                    s.drain()
                    dma_grp(s, "scr", [(scr_xh[iB], xTh[pB_][:]),
                                       (scr_xl[iB], xTl[pB_][:])], f"scrx{iB}")
                if 0 <= iC < NTOT:
                    wv(p, "v", f"feat{iC}")
                    if iC % NT == 0 and iC > 0:
                        wv(p, "v", f"kvs{iC // NT - 1}")
                    jj = iC % NT
                    mm = None
                    for h in range(H):
                        bank = pb[6] if h < 8 else pb[7]
                        off = (h % 8) * 64
                        mm = p.matmul(bank[0:64, off:off + 64],
                                      kq[pC][:, h * 64:(h + 1) * 64],
                                      vex[pC][:, h * 64:(h + 1) * 64],
                                      start=(jj == 0 and h % 8 == 0),
                                      stop=(jj == NT - 1 and h % 8 == 7))
                    _ti(mm, p_m, 1)
                    bump("p", f"kv{iC}")
                    _ti(v.nd().tensor_tensor(ks_acc[:], ks_acc[:], kq[pC][:], ALU.add),
                        v_m, 1)
                    bump("v", f"ksv{iC}")
                if 0 <= iB < NTOT:
                    wv(p, "v", f"q{iB}")
                    wv(p, "s", f"trhe{iB}")
                    mm = None
                    for c in range(FCH):
                        bank = pb[2] if c < 4 else pb[3]
                        mm = p.matmul(bank[:, (c % 4) * 128:(c % 4) * 128 + 128],
                                      qlhs[pB_][:, bass.ts(c, 128)], idn[:],
                                      start=True, stop=True)
                    _ti(mm, p_m, 1)
                    bump("p", f"qT{iB}")
                if 0 <= iB < NTOT:
                    wv(v, "p", f"qT{iB}")
                    v.nd().tensor_copy(qlhs[pB_][:, 0:512], pb[2][:])
                    _ti(v.nd().tensor_copy(qlhs[pB_][:, 512:1024], pb[3][:]), v_m, 1)
                    bump("v", f"qTe{iB}")
                    wv(s, "v", f"qTe{iB}")
                    dma_grp(s, "scr", [(scr_ql[iB], qlhs[pB_][:])], f"scrq{iB}")
                if 0 <= iA < NTOT:
                    wv(s, "v", f"lnf{iA}")
                    s.activation(stat[pA][:, 2:3], stat[pA][:, 1:2], AFT.Ln,
                                 bias=csteps[:], scale=1.0 / D)
                    s.drain()
                    _ti(s.activation(stat[pA][:, 3:4], stat[pA][:, 2:3], AFT.Exp,
                                     scale=-0.5), s_m, 1)
                    bump("s", f"rstd{iA}")
                if 0 <= iB < NTOT:
                    wv(p, "s", f"trhe{iB}")
                    wv(p, "v", f"trl{iB}")
                    for r in range(4):
                        bank = (pb[0], pb[1], pb[4], pb[5])[r]
                        if iB >= 1 and r < 2:
                            wv(p, "s", f"pre{iB - 1}_{r}")
                        half, b2 = (0, r) if r < 2 else (1, r - 2)
                        mm = None
                        for c in range(FCH):
                            for ps_ in range(3):
                                lhsT = (xTh[pB_] if ps_ < 2 else xTl[pB_])[:, bass.ts(c, 128)]
                                wsel = wa if ps_ != 1 else wb
                                base = c * 2048 + half * 1024 + b2 * 512
                                mm = p.matmul(bank[:], lhsT, wsel[:, base:base + 512],
                                              start=(c == 0 and ps_ == 0),
                                              stop=(c == FCH - 1 and ps_ == 2))
                        _ti(mm, p_m, 1)
                        bump("p", f"pr{iB}_{r}")
                        wv(s, "p", f"pr{iB}_{r}")
                        if r < 2:
                            ins = s.activation(kq[pB_][:, 512 * r:512 * (r + 1)], bank[:], AFT.Copy)
                        else:
                            ins = s.activation(vex[pB_][:, 512 * (r - 2):512 * (r - 1)], bank[:], AFT.Copy)
                        _ti(ins, s_m, 1)
                        bump("s", f"pre{iB}_{r}")
                if 0 <= iB < NTOT:
                    wv(g, "s", f"pre{iB}_1")
                    if iB >= 2:
                        wv(g, "v", f"feat{iB - 2}")
                    rope(g, "g", kq[pB_], rot[pB_], iB % NT, f"rope{iB}")
                if 0 <= iB < NTOT:
                    wv(s, "g", f"rope{iB}")
                    if iB >= 2:
                        wv(s, "v", f"feat{iB - 2}")
                    s.activation(ebuf[pB_][:], rot[pB_][:], AFT.Exp)
                    _ti(s.activation(rot[pB_][:], rot[pB_][:], AFT.Relu), s_m, 1)
                    bump("s", f"exp{iB}")
                if 0 <= iA < NTOT:
                    wv(v, "s", f"rstd{iA}")
                    if iA >= 2:
                        wv(v, "p", f"ptrl{iA - 2}")
                    v.nd().tensor_scalar(xnbuf[pA][:], xbuf[pA][:], stat[pA][:, 0:1],
                                         stat[pA][:, 3:4], ALU.subtract, ALU.mult)
                    v.reduce_max(stat[pA][:, 5:6], xnbuf[pA][:], AX.X, apply_absolute_value=True)
                    v.nd().tensor_copy(xhtok[pA][:], xnbuf[pA][:])
                    v.nd().tensor_scalar(ls_sb[:, iA:iA + 1], stat[pA][:, 5:6], 1e-12,
                                         1.0 / 127.0, ALU.max, ALU.mult)
                    v.reciprocal(stat[pA][:, 7:8], ls_sb[:, iA:iA + 1])
                    _ti(v.tensor_tensor(xltok[pA][:], xnbuf[pA][:], xhtok[pA][:], ALU.subtract),
                        v_m, 1)
                    bump("v", f"split{iA}")
                if 0 <= iA < NTOT:
                    if iA >= 2:
                        wd(v, "scr", f"scrq{iA - 2}")
                    v.drain()
                    v.nd().tensor_scalar(upl[:], xnbuf[pA][:], stat[pA][:, 7:8], 1.0, ALU.mult, ALU.mult)
                    rem = peel_core(v, upl[:])
                    v.nd().scalar_tensor_tensor(tpl[:], rem[:], -1.0, upl[:], ALU.mult, ALU.add)
                    _ti(v.nd().tensor_scalar(qlhs[pA][:], tpl[:], 1.0, 0.5, ALU.mult, ALU.add),
                        v_m, 1)
                    bump("v", f"q{iA}")
                if 0 <= iC < NTOT and (iC % NT) == NT - 1:
                    s_idx = iC // NT
                    stg = kvstg
                    wv(s, "p", f"kv{iC}")
                    s.activation(stg[0:64, 0:512], pb[6][0:64, 0:512], AFT.Copy)
                    _ti(s.activation(stg[0:64, 512:1024], pb[7][0:64, 0:512], AFT.Copy),
                        s_m, 1)
                    bump("s", f"kvstg{s_idx}")
                    wv(p, "s", f"kvstg{s_idx}")
                    wv(p, "v", f"ksv{iC}")
                    for pr in range(8):
                        h1 = 2 * pr + 1
                        mm = p.matmul(pb[2][:, pr * 64:(pr + 1) * 64], idsh[:],
                                      stg[0:64, h1 * 64:(h1 + 1) * 64],
                                      start=True, stop=True)
                    for pr in range(8):
                        mm = p.matmul(pb[3][:, pr:pr + 1],
                                      ks_acc[:, pr * 128:(pr + 1) * 128],
                                      ones_col[:, 0:1], start=True, stop=True)
                    _ti(mm, p_m, 1)
                    bump("p", f"kvsh{s_idx}")
                    wv(s, "p", f"kvsh{s_idx}")
                    ins = None
                    for pr in range(8):
                        base = (s_idx * 8 + pr) * 130
                        h0 = 2 * pr
                        c0 = h0 * 64
                        s.activation(kv_h[0:64, base:base + 64],
                                     stg[0:64, c0:c0 + 64], AFT.Copy)
                        s.activation(kv_h[64:128, base + 65:base + 129],
                                     pb[2][64:128, pr * 64:(pr + 1) * 64], AFT.Copy)
                        s.activation(kv_h[0:64, base + 64:base + 65],
                                     pb[3][0:64, pr:pr + 1], AFT.Copy)
                        ins = s.activation(kv_h[64:128, base + 129:base + 130],
                                           pb[3][64:128, pr:pr + 1], AFT.Copy)
                    _ti(ins, s_m, 1)
                    bump("s", f"kvh{s_idx}")
                    wv(v, "s", f"kvh{s_idx}")
                    ins = None
                    for pr in range(8):
                        base = (s_idx * 8 + pr) * 130
                        h0 = 2 * pr
                        c0 = h0 * 64
                        v.tensor_tensor(kv_l[0:64, base:base + 64],
                                        stg[0:64, c0:c0 + 64],
                                        kv_h[0:64, base:base + 64], ALU.subtract)
                        v.tensor_tensor(kv_l[64:128, base + 65:base + 129],
                                        pb[2][64:128, pr * 64:(pr + 1) * 64],
                                        kv_h[64:128, base + 65:base + 129], ALU.subtract)
                        v.tensor_tensor(kv_l[0:64, base + 64:base + 65],
                                        pb[3][0:64, pr:pr + 1],
                                        kv_h[0:64, base + 64:base + 65], ALU.subtract)
                        ins = v.tensor_tensor(kv_l[64:128, base + 129:base + 130],
                                              pb[3][64:128, pr:pr + 1],
                                              kv_h[64:128, base + 129:base + 130], ALU.subtract)
                    v.memset(ks_acc[:], 0.0)
                    _ti(ins, v_m, 1)
                    bump("v", f"kvs{s_idx}")

            for c in range(2 * FCH):
                src = (wqkvT_d[128 * c:128 * (c + 1), 0:D] if c < FCH
                       else woutT_d[128 * (c - FCH):128 * (c - FCH) + 128, :])
                dstc = c * 1024
                if c < 2:
                    wv(y, "p", f"pr{NTOT - 1}_3")
                    wv(y, "v", f"q{NTOT - 1}")
                else:
                    wv(y, "v", f"wb2_{c - 2}")
                dma_grp(y, "w", [(xbuf[c % 2][:], src)], f"w_b{c}")
                wd(s, "w", f"w_b{c}")
                _ti(s.activation(wa[:, dstc:dstc + 1024], xbuf[c % 2][:], AFT.Copy), s_m, 1)
                bump("s", f"wa2_{c}")
                wv(v, "s", f"wa2_{c}")
                _ti(v.tensor_tensor(wb[:, dstc:dstc + 1024], xbuf[c % 2][:],
                                    wa[:, dstc:dstc + 1024], ALU.subtract), v_m, 1)
                bump("v", f"wb2_{c}")

            for it in range(NTOT + 4):
                iL, iP, iM, iR, iF = it, it - 1, it - 2, it - 3, it - 4
                pL, pP, pM, pR, pF = iL % 2, iP % 2, iM % 2, iR % 2, iF % 2
                iX = it - 3
                pX = iX % 2
                if 0 <= iF < NTOT:
                    wd(p, "bx", f"bql{iF}")
                    wv(p, "v", f"resc{min(iF + 1, NTOT - 1)}")
                    if iF >= 1:
                        wv(p, "s", f"ffe{iF - 1}")
                    for r in range(2):
                        bank = pb[7] if r == 0 else pb[0]
                        mm = None
                        for c in range(FCH):
                            mm = p.matmul(bank[:], qlhs[pF][:, bass.ts(c, 128)],
                                          qrhs[:, c * 1024 + r * 512:c * 1024 + r * 512 + 512],
                                          start=(c == 0), stop=(c == FCH - 1))
                        if r == 1:
                            _ti(mm, p_m, 1)
                    bump("p", f"ffn{iF}")
                if 0 <= iF < NTOT:
                    wv(s, "p", f"ffn{iF}")
                    if iF >= 2:
                        wv(s, "v", f"epi{iF - 2}")
                    s.activation(ff_sb[pF][:, 0:512], pb[7][:], AFT.Copy)
                    _ti(s.activation(ff_sb[pF][:, 512:1024], pb[0][:], AFT.Copy), s_m, 1)
                    bump("s", f"ffe{iF}")
                if 0 <= iM < NTOT:
                    wv(p, "v", f"bfeat{iM}")
                    if iM >= 2:
                        wv(p, "s", f"aTh{iM - 2}")
                        wv(p, "v", f"aTl{iM - 2}")
                    mm = None
                    for c in range(FCH):
                        bank = pb[2] if c < 4 else pb[3]
                        mm = p.matmul(bank[:, (c % 4) * 128:(c % 4) * 128 + 128],
                                      kq[pM][:, bass.ts(c, 128)], idnf[:],
                                      start=True, stop=True)
                    _ti(mm, p_m, 1)
                    bump("p", f"qfT{iM}")
                if 0 <= iM < NTOT:
                    wv(s, "p", f"qfT{iM}")
                    if iM >= 2:
                        wv(s, "p", f"attn{iM - 2}")
                    s.activation(qfTh[pM][:, 0:512], pb[2][:], AFT.Copy)
                    _ti(s.activation(qfTh[pM][:, 512:1024], pb[3][:], AFT.Copy), s_m, 1)
                    bump("s", f"qfTh{iM}")
                if 0 <= iM < NTOT:
                    wv(v, "p", f"qfT{iM}")
                    wv(v, "s", f"qfTh{iM}")
                    if iM >= 2:
                        wv(v, "p", f"attn{iM - 2}")
                    v.tensor_tensor(qfTl[pM][:, 0:512], pb[2][:], qfTh[pM][:, 0:512], ALU.subtract)
                    _ti(v.tensor_tensor(qfTl[pM][:, 512:1024], pb[3][:], qfTh[pM][:, 512:1024],
                                        ALU.subtract), v_m, 1)
                    bump("v", f"qfTl{iM}")
                if 0 <= iP < NTOT:
                    wd(p, "bx", f"bxT{iP}")
                    if iP == 0:
                        wv(p, "v", f"wb2_{2 * FCH - 1}")
                    if iP >= 4:
                        wv(p, "s", f"woe{iP - 4}")
                    if iP >= 1:
                        wv(p, "v", f"bevq{iP - 1}")
                    for r in range(2):
                        bank = pb[5 + r]
                        mm = None
                        for c in range(FCH):
                            for ps_ in range(3):
                                lhsT = (xTh[pP] if ps_ < 2 else xTl[pP])[:, bass.ts(c, 128)]
                                wsel = wa if ps_ != 1 else wb
                                base = c * 1024 + r * 512
                                mm = p.matmul(bank[:], lhsT, wsel[:, base:base + 512],
                                              start=(c == 0 and ps_ == 0),
                                              stop=(c == FCH - 1 and ps_ == 2))
                        _ti(mm, p_m, 1)
                        bump("p", f"bpr{iP}_{r}")
                if 0 <= iP < NTOT:
                    if iP >= 2:
                        wv(v, "p", f"qfT{iP - 2}")
                    for r in range(2):
                        wv(v, "p", f"bpr{iP}_{r}")
                        _ti(v.nd().tensor_copy(kq[pP][:, 512 * r:512 * (r + 1)], pb[5 + r][:]), v_m, 1)
                        bump("v", f"bevq{iP}" if r == 1 else f"bevq{iP}_0")
                if 0 <= iP < NTOT:
                    wv(g, "v", f"bevq{iP}")
                    if iP >= 2:
                        wv(g, "v", f"bfeat{iP - 2}")
                    rope(g, "g", kq[pP], rot[pP], iP % NT, f"brope{iP}")
                if 0 <= iP < NTOT:
                    wv(s, "g", f"brope{iP}")
                    if iP >= 2:
                        wv(s, "v", f"bfeat{iP - 2}")
                    s.activation(ebuf[pP][:], rot[pP][:], AFT.Exp, scale=SCALE_Q)
                    _ti(s.activation(rot[pP][:], rot[pP][:], AFT.Relu, scale=SCALE_Q),
                        s_m, 1)
                    bump("s", f"bexp{iP}")
                if 0 <= iM < NTOT:
                    sM = iM // NT
                    wv(p, "v", f"qfTl{iM}")
                    wv(p, "s", f"qfTh{iM}")
                    if 0 <= iF < NTOT:
                        wv(p, "s", f"ffe{iF}")
                    if iM == 0:
                        wv(p, "v", f"kvs{n_seq - 1}")
                    if iM >= 1:
                        wv(p, "v", f"resc{iM - 1}")
                    mm = None
                    for pr in range(8):
                        bank = pb[0] if pr < 3 else (pb[1] if pr < 6 else pb[4])
                        off = (pr % 3) * 130 if pr < 6 else (pr - 6) * 130
                        kvbase = (sM * 8 + pr) * 130
                        for ps_ in range(3):
                            lhsT = (qfTh[pM] if ps_ < 2 else qfTl[pM])[:, bass.ts(pr, 128)]
                            rhs_ = (kv_h if ps_ != 1 else kv_l)[:, kvbase:kvbase + 130]
                            mm = p.matmul(bank[:, off:off + 130], lhsT, rhs_,
                                          start=(ps_ == 0 and pr in (0, 3, 6)),
                                          stop=(ps_ == 2 and pr in (2, 5, 7)))
                    _ti(mm, p_m, 1)
                    bump("p", f"attn{iM}")
                if 0 <= iM < NTOT:
                    wv(v, "p", f"attn{iM}")
                    if iM >= 2:
                        wv(v, "p", f"aT{iM - 2}")
                    dn = stat[pM][:, 8:24]
                    rc = stat[pM][:, 24:40]
                    v.nd().tensor_single_scalar(dn[:, 0:6].rearrange("p (j t) -> p j t", t=2),
                                           bass.AP(pb[0][:].tensor, pb[0][:].offset + 64,
                                                   [list(pb[0][:].ap[0]), [130, 3], [65, 2]]),
                                           1e-6, ALU.max)
                    v.nd().tensor_single_scalar(dn[:, 6:12].rearrange("p (j t) -> p j t", t=2),
                                           bass.AP(pb[1][:].tensor, pb[1][:].offset + 64,
                                                   [list(pb[1][:].ap[0]), [130, 3], [65, 2]]),
                                           1e-6, ALU.max)
                    v.nd().tensor_single_scalar(dn[:, 12:16].rearrange("p (j t) -> p j t", t=2),
                                           bass.AP(pb[4][:].tensor, pb[4][:].offset + 64,
                                                   [list(pb[4][:].ap[0]), [130, 2], [65, 2]]),
                                           1e-6, ALU.max)
                    v.reciprocal(rc[:], dn[:])
                    v.drain()
                    ins = None
                    for bi, (bank, npr, dbase) in enumerate(((pb[0], 3, 0), (pb[1], 3, 3), (pb[4], 2, 6))):
                        num = bass.AP(bank[:].tensor, bank[:].offset,
                                      [list(bank[:].ap[0]), [130, npr], [65, 2], [1, 64]])
                        rcb = bass.AP(rc.tensor, rc.offset + dbase * 2,
                                      [list(rc.ap[0]), [2, npr], [1, 2], [0, 64]])
                        dst = xnbuf[pM][:]
                        dst2 = bass.AP(dst.tensor, dst.offset + dbase * 2 * 64,
                                       [list(dst.ap[0]), [128, npr], [64, 2], [1, 64]])
                        ins = v.tensor_tensor(dst2, num, rcb, ALU.mult)
                    _ti(ins, v_m, 1)
                    bump("v", f"resc{iM}")
                if 0 <= iF < NTOT:
                    wv(p, "s", f"aTh{iF}")
                    wv(p, "v", f"aTl{iF}")
                    wv(p, "v", f"bevq{min(max(iP, 0), NTOT - 1)}")
                    for r in range(2):
                        bank = pb[5 + r]
                        mm = None
                        for c in range(FCH):
                            for ps_ in range(3):
                                lhsT = (xhtok[pF] if ps_ < 2 else xltok[pF])[:, bass.ts(c, 128)]
                                wsel = wa if ps_ != 1 else wb
                                base = 8192 + c * 1024 + r * 512
                                mm = p.matmul(bank[:], lhsT, wsel[:, base:base + 512],
                                              start=(c == 0 and ps_ == 0),
                                              stop=(c == FCH - 1 and ps_ == 2))
                        if r == 1:
                            _ti(mm, p_m, 1)
                    bump("p", f"wout{iF}")
                if 0 <= iF < NTOT:
                    wv(s, "p", f"wout{iF}")
                    if iF >= 2:
                        wv(s, "v", f"epi{iF - 2}")
                    s.activation(vex[pF][:, 0:512], pb[5][:], AFT.Copy)
                    _ti(s.activation(vex[pF][:, 512:1024], pb[6][:], AFT.Copy), s_m, 1)
                    bump("s", f"woe{iF}")
                if 0 <= iR < NTOT:
                    wv(p, "v", f"resc{iR}")
                    if 0 <= iM < NTOT:
                        wv(p, "s", f"qfTh{iM}")
                        wv(p, "v", f"qfTl{iM}")
                    elif iR >= 1:
                        wv(p, "s", f"aTh{iR - 1}")
                        wv(p, "v", f"aTl{iR - 1}")
                    mm = None
                    for c in range(FCH):
                        bank = pb[2] if c < 4 else pb[3]
                        mm = p.matmul(bank[:, (c % 4) * 128:(c % 4) * 128 + 128],
                                      xnbuf[pR][:, bass.ts(c, 128)], idnf[:],
                                      start=True, stop=True)
                    _ti(mm, p_m, 1)
                    bump("p", f"aT{iR}")
                if 0 <= iR < NTOT:
                    wv(s, "p", f"aT{iR}")
                    if iR >= 2:
                        wv(s, "p", f"wout{iR - 2}")
                    s.activation(xhtok[pR][:, 0:512], pb[2][:], AFT.Copy)
                    _ti(s.activation(xhtok[pR][:, 512:1024], pb[3][:], AFT.Copy), s_m, 1)
                    bump("s", f"aTh{iR}")
                if 0 <= iR < NTOT:
                    wv(v, "p", f"aT{iR}")
                    wv(v, "s", f"aTh{iR}")
                    v.tensor_tensor(xltok[pR][:, 0:512], pb[2][:], xhtok[pR][:, 0:512], ALU.subtract)
                    _ti(v.tensor_tensor(xltok[pR][:, 512:1024], pb[3][:], xhtok[pR][:, 512:1024],
                                        ALU.subtract), v_m, 1)
                    bump("v", f"aTl{iR}")
                if 0 <= iP < NTOT:
                    wv(v, "s", f"bexp{iP}")
                    _ti(v.nd().scalar_tensor_tensor(kq[pP][:], ebuf[pP][:], 1.0, rot[pP][:],
                                                    ALU.min, ALU.add), v_m, 1)
                    bump("v", f"bfeat{iP}")
                if 0 <= iF < NTOT:
                    wv(v, "s", f"woe{iF}")
                    wv(v, "s", f"ffe{iF}")
                    wd(v, "bx", f"bxx{iF}")
                    if iF >= 2:
                        wd(v, "out", f"out{iF - 2}")
                    v.nd().scalar_tensor_tensor(tpl[:], ff_sb[pF][:], ls_sb[:, iF:iF + 1],
                                                rs_bc[:], ALU.mult, ALU.mult)
                    v.tensor_tensor(outb[pF][:], xbuf[pF][:], vex[pF][:, 0:D], ALU.add)
                    _ti(v.tensor_tensor(outb[pF][:], outb[pF][:], tpl[:], ALU.add), v_m, 1)
                    bump("v", f"epi{iF}")
                if 0 <= iF < NTOT:
                    wv(s, "v", f"epi{iF}")
                    dma_grp(s, "out", [(out_d[TILE * iF:TILE * (iF + 1), :], outb[pF][:])],
                            f"out{iF}")
                if 0 <= iL < NTOT:
                    if iL >= 2:
                        wv(y, "p", f"bpr{iL - 2}_1")
                    else:
                        lastp = NTOT - 1 - ((NTOT - 1 - iL) % 2)
                        wv(y, "p", f"pr{lastp}_3")
                        wd(y, "scr", f"scrx{lastp}")
                    dma_grp(y, "bx", [(xTh[pL][:], scr_xh[iL]),
                                      (xTl[pL][:], scr_xl[iL])], f"bxT{iL}")
                if 0 <= iX < NTOT:
                    if iX >= 2:
                        wv(y, "v", f"epi{iX - 2}")
                        wv(y, "p", f"ffn{iX - 2}")
                    else:
                        lastp = NTOT - 1 - ((NTOT - 1 - iX) % 2)
                        wv(y, "v", f"wb2_{2 * FCH - 2 + iX}")
                        wd(y, "scr", f"scrq{lastp}")
                    dma_grp(y, "bx", [(xbuf[pX][:], x_d[TILE * iX:TILE * (iX + 1), :])], f"bxx{iX}")
                    dma_grp(y, "bx", [(qlhs[pX][:], scr_ql[iX])], f"bql{iX}")
            y.wait_ge(s_out, DM["out"])
            return C, DM

        @block.sync
        def _(sync):
            walk("y", sync)

        @block.vector
        def _(vector):
            walk("v", vector)

        @block.scalar
        def _(scalar):
            walk("s", scalar)

        @block.tensor
        def _(tensor):
            walk("p", tensor)

        @block.gpsimd
        def _(gpsimd):
            walk("g", gpsimd)

    return nc


def kernel(L0_attn_x, L0_attn_w_qkv, L0_attn_w_out, L0_ffn_rhs, _nt=T // TILE):
    x = np.ascontiguousarray(np.asarray(L0_attn_x, dtype=np.float32))
    wqkvT = np.ascontiguousarray(np.asarray(L0_attn_w_qkv, dtype=np.float32).T)
    woutT = np.ascontiguousarray(np.asarray(L0_attn_w_out, dtype=np.float32).T)
    rhs = np.ascontiguousarray(np.asarray(L0_ffn_rhs, dtype=np.float32))
    rhsT = np.ascontiguousarray(rhs.T)
    cos, sin = _host_tables(_nt)
    idn = np.eye(128, dtype=ml_dtypes.bfloat16)
    idnf = np.eye(128, dtype=np.float32)
    idsh = np.eye(64, 128, k=64, dtype=np.float32)
    nb, tt, dd = x.shape
    n_seq = nb // NCORES
    nc = build_nc(NT=_nt, n_seq=n_seq)
    in_maps = []
    for c in range(NCORES):
        xc = np.ascontiguousarray(x[c * n_seq:(c + 1) * n_seq].reshape(n_seq * tt, dd))
        in_maps.append({"x": xc, "wqkvT": wqkvT, "woutT": woutT, "rhs": rhs,
                        "rhsT": rhsT, "cos": cos, "sin": sin, "idn": idn,
                        "idnf": idnf, "idsh": idsh})
    res = run_bass_kernel_spmd(nc, in_maps, core_ids=list(range(NCORES)))
    out = np.concatenate([r["out"].reshape(n_seq, tt, dd) for r in res.results], axis=0)
    return out
